# version 19 (final)
# speedup vs baseline: 1.4512x; 1.0103x over previous
import numpy as np
import ml_dtypes
import concourse.bass as bass
import concourse.mybir as mybir
from concourse.bass_utils import run_bass_kernel_spmd

F32 = mybir.dt.float32
BF16 = mybir.dt.bfloat16
I32 = mybir.dt.int32
AF = mybir.ActivationFunctionType
ALU = mybir.AluOpType

B, L, D = 4, 4096, 2048
NST = 16
INTER = 5504
KCI = 43
CH = 512
NCH = L // CH
HT = 256
DC = D // 128
EH = D // 2
DCH = EH // 128
NCORE = 8
TILE = 128 * 128
GT_PER = 2 * KCI
BLOB = 2 * GT_PER + 6 * DC
SLUT = 16

ENGS = ("tensor", "vector", "scalar", "gpsimd", "sync")
import os as _os
RNCH = int(_os.environ.get("KNCH", NCH))


def build_program():
    nc = bass.Bass("TRN2", target_bir_lowering=False)

    def din(name, shape, dtype):
        return nc.dram_tensor(name, shape, dtype, kind="ExternalInput")

    xT = din("xT", [D, L], F32)
    xDr = din("xD", [D, NCH * HT], F32)
    dtpwT = din("dtpwT", [128, EH], F32)
    xpw = din("xpw", [128, D], F32)
    wbcT = din("wbcT", [D, 2 * NST], F32)
    dtbias = din("dtbias", [EH], F32)
    alog = din("alog", [EH, NST], F32)
    dvec = din("dvec", [D], F32)
    opwT = din("opwT", [D, D], F32)
    opbias = din("opbias", [D], F32)
    gidx = din("gidx", [INTER, 256], I32)
    uidx = din("uidx", [INTER, 256], I32)
    didx = din("didx", [D, 768], I32)
    lutd = din("lut", [SLUT], F32)
    wmod_s = din("wmod_s", [256, 6 * D], F32)
    cTd = din("cT", [256, B], F32)
    identd = din("ident", [128, 128], BF16)
    masksd = din("masks", [128, 8], F32)
    x3d = nc.dram_tensor("x3", [D, NCH * HT], F32, kind="ExternalOutput")

    statd = nc.dram_tensor("statd", [2, CH], F32)
    statd2 = nc.dram_tensor("statd2", [2, HT], F32)
    bcd = nc.dram_tensor("bcd", [2 * NST, CH], BF16)
    modp = nc.dram_tensor("modp", [B, 6 * D], F32)
    modr = nc.dram_tensor("modr", [B, 6 * D], F32)
    wag_in = nc.dram_tensor("wag_in", [BLOB * TILE], BF16)
    wag_out = nc.dram_tensor("wag_out", [NCORE * BLOB * TILE], BF16)
    wg_d = nc.dram_tensor("wg_d", [KCI * DC * TILE], BF16)
    wu_d = nc.dram_tensor("wu_d", [KCI * DC * TILE], BF16)
    wd_d = nc.dram_tensor("wd_d", [DC * 48 * TILE], BF16)
    dtw_d = nc.dram_tensor("dtw_d", [DC * DCH * TILE], BF16)
    opw_d = nc.dram_tensor("opw_d", [DC * DC * TILE], BF16)
    rs_in = [nc.dram_tensor(f"rs_in{i}", [2 * D, HT], BF16) for i in range(2)]
    rs_out = [nc.dram_tensor(f"rs_out{i}", [D, HT], BF16) for i in range(NCH)]
    x2d = nc.dram_tensor("x2d", [D, NCH * HT], F32)

    from contextlib import ExitStack
    es = ExitStack()

    def sbt(name, shape, dtype):
        return es.enter_context(nc.sbuf_tensor(name, shape, dtype))

    constp = sbt("constp", [128, 4], F32)
    identt = sbt("identt", [128, 128], BF16)
    onesb = sbt("onesb", [128, 1], BF16)
    maskt = sbt("maskt", [128, 8], F32)
    modg = sbt("modg", [128, 6 * DC], F32)
    modg2 = sbt("modg2", [128, 6 * DC], F32)
    mods = sbt("mods", [128, 2 * DC], F32)
    cap1g = sbt("cap1g", [128, DC], F32)
    gacol = sbt("gacol", [128, DC], F32)
    gbcol = sbt("gbcol", [128, DC], F32)
    cbp1 = sbt("cbp1", [128, DC], F32)
    dvecc = sbt("dvecc", [128, DC], F32)
    opbc = sbt("opbc", [128, DC], F32)
    dtbc = sbt("dtbc", [128, DCH], F32)
    acol = sbt("acol", [128, DCH * NST], F32)
    lutc = sbt("lutc", [128, SLUT], F32)
    hst = sbt("hst", [128, DCH * NST], F32)
    ctmp = sbt("ctmp", [128, 2 * B], F32)
    cslc = sbt("cslc", [128, 2 * B], F32)
    wbc_b = sbt("wbc_b", [128, DC * 2 * NST], BF16)
    xpw_b = sbt("xpw_b", [128, D], BF16)
    dtpw_b = sbt("dtpw_b", [128, EH], BF16)
    modsb = sbt("modsb", [4, CH], F32)
    wmodb = sbt("wmodb", [128, CH], F32)
    clamps = sbt("clamps", [128, 3 * DC], F32)
    clampi = sbt("clampi", [128, DC], I32)
    xbt = [sbt(f"xbt{i}", [128, CH], BF16) for i in range(2)]
    sqt = [sbt(f"sqt{i}", [128, CH], BF16) for i in range(2)]
    hb = sbt("hb", [128, DC * CH], BF16)
    statr = sbt("statr", [1, 2 * CH], F32)
    srow = sbt("srow", [1, 2 * CH], F32)
    RM = sbt("RM", [128, 2 * CH], F32)
    lnw0 = sbt("lnw0", [128, CH], F32)
    delta = sbt("delta", [128, DCH * CH], F32)
    wsc = sbt("wsc", [128, DCH * CH], F32)
    etmp1 = sbt("etmp1", [128, CH], F32)
    etmp = [etmp1, etmp1]
    bcv = sbt("bcv", [32, CH], BF16)
    BbCb = [sbt(f"BbCb{i}", [128, 2 * CH], BF16) for i in range(3)]
    abuf = [sbt(f"abuf{i}", [128, CH], F32) for i in range(2)]
    ubuf = [sbt(f"ubuf{i}", [128, CH], F32) for i in range(2)]
    Hb1 = sbt("Hb1", [128, CH], F32)
    Hbuf = [Hb1, Hb1]
    yacc = sbt("yacc", [128, DCH * CH], F32)
    dtw_b1 = sbt("dtw_b1", [128, DC * 128], BF16)
    dtw_b = [dtw_b1, dtw_b1]
    yDt = sbt("yDt", [128, DC * HT], BF16)
    xbD = sbt("xbD", [128, DC * HT], BF16)
    xdf = [sbt(f"xdf{i}", [128, HT], F32) for i in range(2)]
    m1t = [sbt(f"m1t{i}", [128, HT], F32) for i in range(2)]
    hDt = sbt("hDt", [128, HT], F32)
    x2t = [sbt(f"x2t{i}", [128, HT], F32) for i in range(2)]
    xb2 = [sbt(f"xb2_{i}", [128, HT], BF16) for i in range(2)]
    xh2 = [sbt(f"xh2_{i}", [128, HT], BF16) for i in range(2)]
    h2b = sbt("h2b", [128, DC * HT], BF16)
    statr2 = sbt("statr2", [1, 2 * HT], F32)
    RM2 = sbt("RM2", [128, 2 * HT], F32)
    opw_b1 = sbt("opw_b1", [128, DC * 128], BF16)
    opw_b = [opw_b1, opw_b1]
    wf_b = [sbt(f"wf_b{i}", [128, DC * 128], BF16) for i in range(3)]
    gtt = [sbt(f"gtt{i}", [128, HT], F32) for i in range(2)]
    hsb = sbt("hsb", [128, KCI * HT], BF16)
    x2s = [sbt(f"x2s{i}", [128, HT], F32) for i in range(2)]
    x3s = sbt("x3s", [128, 2 * HT], F32)
    WPN = 512
    wpidx = sbt("wpidx", [128, WPN], I32)
    class _CV:
        def __init__(self, t, off):
            self.t, self.off = t, off

        def __getitem__(self, idx):
            r, c = idx
            return self.t[r, c.start + self.off:c.stop + self.off]
    wpidxb = _CV(xpw_b, 0)
    wpm1 = _CV(xpw_b, 512)
    wpmask = [wpm1, wpm1]
    wpacc = _CV(xpw_b, 1024)
    wpstage = sbt("wpstage", [128, WPN], BF16)
    pb = [es.enter_context(nc.psum_tensor(f"pb{i}", [128, 512], F32)) for i in range(7)]
    pbT = es.enter_context(nc.psum_tensor("pbT", [128, 1024], BF16))

    nc.const_aps.register(F32, 1.0, constp[:, 0:1])
    nc.const_aps.register(F32, 1e-5, constp[:, 1:2])
    nc.const_aps.register(F32, -0.5, constp[:, 2:3])
    nc.const_aps.register(F32, 0.0, constp[:, 3:4])

    sems = {n: es.enter_context(nc.semaphore(n))
            for n in ["sT", "sV", "sS", "sG", "sCC", "sST"]}
    em_sems = sems

    ops = {e: [] for e in ENGS}
    cnt = {n: 0 for n in sems}
    mile = {}

    DMA_SLOT = {
        "xbs": lambda k: k[2] % 2, "xbh": lambda k: k[2] % 2,
        "BbCb": lambda k: k[2] % 2,
        "xdf": lambda k: k[2] % 2, "x2ld": lambda k: k[2] % 2,
        "wfld": lambda k: k[2] % 2, "wdld": lambda k: k[3] % 2,
        "xh2": lambda k: k[2] % 2,
    }

    def map_dma_sem(sem, key):
        if sem not in ("sSY", "sGD"):
            return sem
        assert key is not None, "DMA op needs a key for stream mapping"
        pref = key[0] if isinstance(key, tuple) else key
        slot = DMA_SLOT.get(pref)
        name = "d_" + str(pref) + (str(slot(key)) if slot else "")
        if name not in cnt:
            cnt[name] = 0
            if em_sems is not None:
                em_sems[name] = es.enter_context(nc.semaphore(name))
        return name

    class Eng:
        def __init__(self, name):
            self.name = name

        def do(self, fn, sem=None, key=None, n=1):
            sem = map_dma_sem(sem, key)
            if sem is not None:
                cnt[sem] += n
                if key is not None:
                    mile[(sem, key)] = cnt[sem]

                def cl(e, fn=fn, sem=sem, n=n):
                    fn(e).then_inc(sems[sem], n)
                ops[self.name].append(cl)
            else:
                ops[self.name].append(lambda e, fn=fn: fn(e))

        def wait(self, sem, key):
            sem = map_dma_sem(sem, key)
            assert (sem, key) in mile, (sem, key)

            def cl(e, sem=sem, key=key):
                e.wait_ge(sems[sem], mile[(sem, key)])
            ops[self.name].append(cl)

        def wait_val(self, sem, val):
            def cl(e, sem=sem, val=val):
                e.wait_ge(sems[sem], val)
            ops[self.name].append(cl)

    T, V, S, G, SY = (Eng(e) for e in ENGS)
    ALLE = {"tensor": T, "vector": V, "scalar": S, "gpsimd": G, "sync": SY}
    ESEM = {"tensor": "sT", "vector": "sV", "scalar": "sS", "gpsimd": "sG", "sync": "sSY"}
    import os as _os
    SAFE = _os.environ.get("KSAFE", "0") == "1"
    bar_i = [0]

    def barrier():
        bar_i[0] += 1
        key = ("bar", bar_i[0])
        for en, E in ALLE.items():
            E.do(lambda e: e.nop(), ESEM[en], key=(key, en))
        for en, E in ALLE.items():
            for en2 in ALLE:
                if en2 != en:
                    E.wait(ESEM[en2], (key, en2))

    def AP(h, off, dims):
        return bass.AP(h, off, dims)

    V.do(lambda e: e.memset(constp[:, 0:1], 1.0))
    V.do(lambda e: e.memset(constp[:, 1:2], 1e-5))
    V.do(lambda e: e.memset(constp[:, 2:3], -0.5))
    V.do(lambda e: e.memset(constp[:, 3:4], 0.0))
    V.do(lambda e: e.memset(onesb[:, :], 1.0), "sV", key="consts")

    SY.do(lambda e: e.dma_start(out=identt[:, :], in_=identd[:, :]), "sSY", key=("p0small", 0), n=16)
    SY.do(lambda e: e.dma_start(out=maskt[:, :], in_=masksd[:, :]), "sSY", key=("p0small", 1), n=16)
    SY.do(lambda e: e.dma_start(out=lutc[:, :], in_=AP(lutd, 0, [[0, 128], [1, SLUT]])), "sSY", key=("p0small", 2), n=16)
    SY.do(lambda e: e.dma_start(out=dtbc[:, :], in_=AP(dtbias, 0, [[1, 128], [128, DCH]]), allow_slow_non_contiguous=True), "sSY", key=("p0small", 3), n=16)
    SY.do(lambda e: e.dma_start(out=acol[:, :], in_=AP(alog, 0, [[NST, 128], [128 * NST, DCH], [1, NST]])), "sSY", key=("p0small", 4), n=16)
    SY.do(lambda e: e.dma_start(out=dvecc[:, :], in_=AP(dvec, 0, [[1, 128], [128, DC]]), allow_slow_non_contiguous=True), "sSY", key=("p0small", 5), n=16)
    SY.do(lambda e: e.dma_start(out=opbc[:, :], in_=AP(opbias, 0, [[1, 128], [128, DC]]), allow_slow_non_contiguous=True), "sSY", key=("p0small", 6), n=16)
    SY.do(lambda e: e.dma_start(out=ctmp[:, :], in_=AP(cTd, 0, [[B, 128], [128 * B, 2], [1, B]])),
          "sSY", key=("p0small", 7), n=16)
    G.do(lambda e: e.dma_start(out=wbc_b[:, :],
                               in_=AP(wbcT, 0, [[2 * NST, 128], [128 * 2 * NST, DC], [1, 2 * NST]])),
         "sGD", key=("p0cast", 0), n=16)
    G.do(lambda e: e.dma_start(out=xpw_b[:, :], in_=xpw[:, :]), "sGD", key=("p0cast", 1), n=16)
    G.do(lambda e: e.dma_start(out=dtpw_b[:, :], in_=dtpwT[:, :]), "sGD", key=("p0cast", 2), n=16)

    S.wait("sSY", ("p0small", 7))
    S.wait("sV", "consts")
    S.do(lambda e: e.activation(out=acol[:, :], in_=acol[:, :], func=AF.Exp), "sS")
    V.wait_val("sS", cnt["sS"])
    V.do(lambda e: e.tensor_scalar_mul(acol[:, :], acol[:, :], -1.0), "sV", key="acol")
    S.wait("sV", "acol")
    S.do(lambda e: e.activation(out=cslc[:, :], in_=ctmp[:, :], func=AF.Silu), "sS", key="cslc")

    for nb in range(24):
        bsl = nb % 2
        for r2 in range(2):
            if nb == 0 and r2 == 0:
                G.wait("sGD", ("p0cast", 2))
            G.wait("sT", ("modmm", nb - 1, 1) if r2 == 0 else ("modmm", nb, 0)) if (nb + r2 > 0) else None
            G.do(lambda e, nb=nb, r2=r2: e.dma_start(
                out=wmodb[:, :],
                in_=wmod_s[r2 * 128:(r2 + 1) * 128, nb * CH:(nb + 1) * CH]),
                "sGD", key=("wmod", nb, r2), n=16)
            T.wait("sGD", ("wmod", nb, r2))
            T.wait("sS", "cslc") if (nb == 0 and r2 == 0) else None
            T.wait("sV", ("modev", nb - 1)) if (nb >= 1 and r2 == 0) else None
            T.do(lambda e, r2=r2: e.matmul(
                pb[5][0:B, :], cslc[:, r2 * B:(r2 + 1) * B], wmodb[:, :],
                start=(r2 == 0), stop=(r2 == 1)), "sT", key=("modmm", nb, r2))
        V.wait("sT", ("modmm", nb, 1))
        V.wait("sSY", ("modst", nb - 1)) if nb >= 1 else None
        V.do(lambda e: e.tensor_copy(out=modsb[:, :], in_=pb[5][0:B, :]), "sV", key=("modev", nb))
        SY.wait("sV", ("modev", nb))
        SY.do(lambda e, nb=nb: e.dma_start(
            out=AP(modp, nb * CH, [[6 * D, B], [1, CH]]), in_=modsb[:, :]),
            "sSY", key=("modst", nb), n=16)
    G.wait("sSY", ("modst", 23))
    G.do(lambda e: e.collective_compute(
        "AllReduce", ALU.add, replica_groups=[list(range(NCORE))],
        ins=[modp.ap()], outs=[modr.ap()]), "sCC", key="modAR")
    SY.wait("sCC", "modAR")
    for bi in range(4):
        if bi >= 1:
            SY.wait("sV", ("modgb", bi - 1))
        SY.do(lambda e, bi=bi: e.dma_start(
            out=modg2[:, :],
            in_=AP(modr, bi * 6 * D, [[1, 128], [D, 6], [128, DC]]),
            allow_slow_non_contiguous=True),
            "sSY", key=("modrow", bi), n=16)
        V.wait("sSY", ("modrow", bi))
        if bi == 0:
            V.do(lambda e: e.tensor_scalar_mul(modg[:, :], modg2[:, :], maskt[:, 2:3]),
                 "sV", key=("modgb", 0))
        else:
            V.do(lambda e, bi=bi: e.scalar_tensor_tensor(
                out=modg[:, :], in0=modg2[:, :],
                scalar=maskt[:, 2 + bi:3 + bi], in1=modg[:, :],
                op0=ALU.mult, op1=ALU.add), "sV", key=("modgb", bi) if bi < 3 else "modg")
    V.wait("sV", "modg")
    for hf in range(2):
        V.do(lambda e, hf=hf: e.tensor_copy(
            out=AP(modg2, hf * 8, [[6 * DC, 128], [DC, 6], [1, 8]]),
            in_=AP(modg, (1 - hf) * 8, [[6 * DC, 128], [DC, 6], [1, 8]])))
    V.do(lambda e: e.tensor_scalar_mul(mods[:, 0:DC], modg[:, DC:2 * DC], maskt[:, 6:7]))
    V.do(lambda e: e.scalar_tensor_tensor(
        out=mods[:, 0:DC], in0=modg2[:, DC:2 * DC], scalar=maskt[:, 7:8],
        in1=mods[:, 0:DC], op0=ALU.mult, op1=ALU.add))
    V.do(lambda e: e.tensor_scalar_add(mods[:, 0:DC], mods[:, 0:DC], 1.0))
    V.do(lambda e: e.tensor_scalar_mul(mods[:, DC:2 * DC], modg[:, 0:DC], maskt[:, 6:7]))
    V.do(lambda e: e.scalar_tensor_tensor(
        out=mods[:, DC:2 * DC], in0=modg2[:, 0:DC], scalar=maskt[:, 7:8],
        in1=mods[:, DC:2 * DC], op0=ALU.mult, op1=ALU.add), "sV", key="mods")
    V.do(lambda e: e.tensor_scalar_add(cap1g[:, :], modg[:, DC:2 * DC], 1.0))
    V.do(lambda e: e.tensor_scalar_add(cbp1[:, :], modg[:, 4 * DC:5 * DC], 1.0))
    for dst, sv in ((gacol, 2), (gbcol, 5)):
        V.do(lambda e, sv=sv: e.tensor_scalar_max(
            clamps[:, 0:DC], modg[:, sv * DC:(sv + 1) * DC], 0.25))
        V.do(lambda e, sv=sv: e.tensor_scalar_min(
            clamps[:, DC:2 * DC], modg[:, sv * DC:(sv + 1) * DC], -0.25))
        V.do(lambda e, sv=sv: e.tensor_single_scalar(
            out=clampi[:, :], in_=modg[:, sv * DC:(sv + 1) * DC],
            scalar=0.0, op=ALU.is_ge))
        V.do(lambda e, dst=dst: e.select(
            out=dst[:, :], mask=clampi[:, :],
            on_true=clamps[:, 0:DC], on_false=clamps[:, DC:2 * DC]),
            "sV", key=("clamp", sv))

    for dc in range(DC):
        for eN in range(2):
            j = dc * 2 + eN
            T.wait("sGD", ("p0cast", 2)) if j == 0 else None
            T.wait("sS", ("dtwev", j - 1)) if j >= 1 else None
            T.do(lambda e, dc=dc, eN=eN: e.matmul(
                pb[6][:, :], xpw_b[:, dc * 128:(dc + 1) * 128],
                dtpw_b[:, eN * 512:(eN + 1) * 512],
                start=True, stop=True), "sT", key=("dtwmm", j))
            S.wait("sT", ("dtwmm", j))
            S.wait("sSY", ("dtwst", j - 1)) if j >= 1 else None
            S.do(lambda e: e.copy(
                out=AP(wpstage, 0, [[WPN, 128], [1, 512]]), in_=pb[6][:, :]),
                "sS", key=("dtwev", j))
            SY.wait("sS", ("dtwev", j))
            SY.do(lambda e, dc=dc, eN=eN: e.dma_start(
                out=AP(dtw_d, (dc * DCH + eN * 4) * TILE,
                       [[128, 128], [TILE, 4], [1, 128]]),
                in_=AP(wpstage, 0, [[WPN, 128], [128, 4], [1, 128]])),
                "sSY", key=("dtwst", j), n=16)

    for kc in range(DC):
        osl = kc % 2
        G.wait("sSY", ("opwst", kc - 1)) if kc >= 1 else None
        G.do(lambda e, kc=kc, osl=osl: e.dma_start(
            out=opw_b[osl][:, :], in_=opwT[kc * 128:(kc + 1) * 128, :]),
            "sGD", key=("opwld", kc), n=16)
        SY.wait("sGD", ("opwld", kc))
        SY.do(lambda e, kc=kc, osl=osl: e.dma_start(
            out=AP(opw_d, kc * 128, [[DC * 128, 128], [DC * TILE, DC], [1, 128]]),
            in_=AP(opw_b[osl], 0, [[DC * 128, 128], [128, DC], [1, 128]])),
            "sSY", key=("opwst", kc), n=16)

    wpk = [0]

    def wprep_piece(srcap_fn, nsub, offs):
        pid = wpk[0]
        wpk[0] += 1
        npc = nsub * 128
        G.wait("sV", ("wpm", (pid - 1, SLUT - 1)) if False else ("wpm", pid - 1, SLUT - 1)) if pid >= 1 else None
        G.do(lambda e, npc=npc: e.dma_start(
            out=AP(wpidx, 0, [[WPN, 128], [1, npc]]), in_=srcap_fn()),
            "sGD", key=("wpld", pid), n=16)
        V.wait("sGD", ("wpld", pid))
        if pid == 0:
            V.wait("sT", ("dtwmm", 2 * DC - 1))
        V.do(lambda e, npc=npc: e.tensor_copy(out=wpidxb[:, 0:npc], in_=wpidx[:, 0:npc]))
        for k in range(SLUT):
            ms = wpmask[k % 2]
            V.do(lambda e, ms=ms, k=k, npc=npc: e.tensor_single_scalar(
                out=ms[:, 0:npc], in_=wpidxb[:, 0:npc], scalar=float(k), op=ALU.is_equal),
                "sV", key=("wpm", pid, k))
            if k == 0:
                V.wait("sT", ("wptr", (pid - 1, "last"))) if pid >= 1 else None
                V.do(lambda e, ms=ms, k=k, npc=npc: e.tensor_scalar_mul(
                    wpacc[:, 0:npc], ms[:, 0:npc], lutc[:, k:k + 1]),
                    "sV", key=("wpa", pid, k))
            else:
                V.do(lambda e, ms=ms, k=k, npc=npc: e.scalar_tensor_tensor(
                    out=wpacc[:, 0:npc], in0=ms[:, 0:npc], scalar=lutc[:, k:k + 1],
                    in1=wpacc[:, 0:npc], op0=ALU.mult, op1=ALU.add),
                    "sV", key=("wpa", pid, k))
        for s in range(nsub):
            T.wait("sV", ("wpa", pid, SLUT - 1)) if s == 0 else None
            if pid + s > 0:
                T.wait("sS", ("wpev", (pid, s - 1)) if s > 0 else ("wpev", (pid - 1, "last")))
            T.do(lambda e, s=s: e.transpose(
                pbT[:, (s % 8) * 128:(s % 8 + 1) * 128],
                wpacc[:, s * 128:(s + 1) * 128], identt[:, :]),
                "sT", key=("wptr", pid, s))
            S.wait("sT", ("wptr", pid, s))
            if pid == 0 and s == 0:
                S.wait("sSY", ("dtwst", 2 * DC - 1))
            if pid >= 1 and s == 0:
                S.wait("sSY", ("wpst", (pid - 1, "last")))
            S.do(lambda e, s=s: e.copy(
                out=wpstage[:, s * 128:(s + 1) * 128],
                in_=pbT[:, (s % 8) * 128:(s % 8 + 1) * 128]),
                "sS", key=("wpev", (pid, s)))
        mile[("sS", ("wpev", (pid, "last")))] = cnt["sS"]
        mile[("sT", ("wptr", (pid, "last")))] = cnt["sT"]
        for s in range(nsub):
            SY.wait("sS", ("wpev", (pid, s)))
            SY.do(lambda e, off=offs[s], s=s: e.dma_start(
                out=AP(wag_in, off * TILE, [[128, 128], [1, 128]]),
                in_=AP(wpstage, s * 128, [[WPN, 128], [1, 128]])),
                "sSY", key=("wpst", (pid, s)), n=16)
        mile[("d_wpst", ("wpst", (pid, "last")))] = cnt["d_wpst"]

    for src, base in ((gidx, 0), (uidx, GT_PER)):
        for icp in range(0, KCI, 2):
            nic = min(2, KCI - icp)
            offs = []
            for icL in range(nic):
                for dcL in range(2):
                    offs.append(base + dcL * KCI + icp + icL)
            wprep_piece(
                lambda src=src, icp=icp, nic=nic: AP(
                    src, icp * 128 * 256,
                    [[256, 128], [128 * 256, nic], [128, 2], [1, 128]]),
                nic * 2, offs)
    for dc in range(DC):
        for kp in range(2):
            offs = [2 * GT_PER + (kp * 3 + kcL) * DC + dc for kcL in range(3)]
            wprep_piece(
                lambda dc=dc, kp=kp: AP(
                    didx, dc * 128 * 768 + kp * 3 * 128,
                    [[768, 128], [128, 3], [1, 128]]),
                3, offs)
    NPIECE = wpk[0]

    G.wait("sSY", ("wpst", (NPIECE - 1, "last")))
    G.do(lambda e: e.collective_compute(
        "AllGather", ALU.bypass, replica_groups=[list(range(NCORE))],
        ins=[wag_in.ap()], outs=[wag_out.ap()]), "sCC", key="wAG")
    _shuf_state = [0]

    def emit_shuf_group(grp):
        if _shuf_state[0] == 0:
            SY.wait("sCC", "wAG")
        _shuf_state[0] += 1
        if grp < 2:
            mi, wdst = (0, wg_d) if grp == 0 else (1, wu_d)
            for cc in range(8):
                for dcL in range(2):
                    kc = 2 * cc + dcL
                    SY.do(lambda e, mi=mi, wdst=wdst, cc=cc, dcL=dcL, kc=kc: e.dma_start(
                        out=AP(wdst, kc * 128,
                               [[DC * TILE, KCI], [DC * 128, 128], [1, 128]]),
                        in_=AP(wag_out, (cc * BLOB + mi * GT_PER + dcL * KCI) * TILE,
                               [[TILE, KCI], [128, 128], [1, 128]])),
                        "sSY", key=("shuf", mi, 2 * cc + dcL), n=16)
        else:
            for cc in range(8):
                for kcL in range(6):
                    kc = cc * 6 + kcL
                    if kc >= KCI:
                        continue
                    SY.do(lambda e, cc=cc, kcL=kcL, kc=kc: e.dma_start(
                        out=AP(wd_d, kc * 128,
                               [[KCI * TILE, DC], [KCI * 128, 128], [1, 128]]),
                        in_=AP(wag_out, (cc * BLOB + 2 * GT_PER + kcL * DC) * TILE,
                               [[TILE, DC], [128, 128], [1, 128]])),
                        "sSY", key=("shuf", 2, kc), n=16)

    x2sl = [x2s[0], x2s[1], wmodb, RM]
    abuf.append(RM2)
    ubuf.append(x3s)

    def lin(ch, a, b, size):
        t = ch * size + a - b
        if t < 0:
            return None
        return (t // size, t % size)

    def emit_stats(ch):
        par = ch % 2
        for dc in range(DC):
            sl = dc % 2
            pk = lin(ch, dc, 2, DC)
            if pk is not None:
                G.wait("sT", ("s1mm", pk[0], pk[1]))
                G.wait("sS", ("sq", pk[0], pk[1]))
                if pk[1] >= DC - 2:
                    G.wait("sV", ("hb", pk[0], pk[1]))
            if dc == 0 and ch >= 1:
                G.wait("sV", ("hb", ch - 1, DC - 1))
                G.wait("sV", ("hb", ch - 1, DC - 2))
            G.do(lambda e, ch=ch, dc=dc, sl=sl: e.dma_start(
                out=xbt[sl][:, :],
                in_=AP(xT, dc * 128 * L + ch * CH, [[L, 128], [1, CH]])),
                "sGD", key=("xbs", ch, dc), n=16)
            T.wait("sGD", ("xbs", ch, dc))
            if dc == 0 and ch >= 1:
                T.wait("sV", ("var", ch - 1))
            T.do(lambda e, dc=dc, sl=sl: e.matmul(
                pb[0][0:1, :], onesb[:, :], xbt[sl][:, :],
                start=(dc == 0), stop=(dc == DC - 1)), "sT", key=("s1mm", ch, dc))
            S.wait("sGD", ("xbs", ch, dc))
            pk = lin(ch, dc, 2, DC)
            if pk is not None:
                S.wait("sT", ("s2mm", pk[0], pk[1]))
            S.do(lambda e, dc=dc, sl=sl: e.activation(
                out=sqt[sl][:, :], in_=xbt[sl][:, :], func=AF.Square),
                "sS", key=("sq", ch, dc))
            T.wait("sS", ("sq", ch, dc))
            T.do(lambda e, dc=dc, sl=sl: e.matmul(
                pb[1][0:1, :], onesb[:, :], sqt[sl][:, :],
                start=(dc == 0), stop=(dc == DC - 1)), "sT", key=("s2mm", ch, dc))
        V.wait("sT", ("s1mm", ch, DC - 1))
        V.wait("sT", ("s2mm", ch, DC - 1))
        if ch >= 1:
            V.wait("sSY", ("RMld", ch - 1))
            V.wait("sS", ("rstd", ch - 1))
        V.do(lambda e: e.tensor_scalar_mul(srow[0:1, 0:CH], pb[0][0:1, :], 1.0 / D))
        V.do(lambda e: e.tensor_scalar_mul(srow[0:1, CH:2 * CH], pb[1][0:1, :], 1.0 / D))
        V.do(lambda e: e.tensor_mul(statr[0:1, CH:2 * CH], srow[0:1, 0:CH], srow[0:1, 0:CH]))
        V.do(lambda e: e.tensor_sub(srow[0:1, CH:2 * CH], srow[0:1, CH:2 * CH],
                                    statr[0:1, CH:2 * CH]),
             "sV", key=("var", ch))
        S.wait("sV", ("var", ch))
        if ch >= 1:
            S.wait("sSY", ("strst", ch - 1))
        S.do(lambda e: e.activation(out=statr[0:1, 0:CH], in_=srow[0:1, CH:2 * CH],
                                    func=AF.Ln, bias=1e-5))
        S.do(lambda e: e.activation(out=statr[0:1, 0:CH], in_=statr[0:1, 0:CH],
                                    func=AF.Exp, scale=-0.5),
             "sS", key=("rstd", ch))
        V.wait("sS", ("rstd", ch))
        V.do(lambda e: e.tensor_mul(statr[0:1, CH:2 * CH], srow[0:1, 0:CH], statr[0:1, 0:CH]),
             "sV", key=("strow", ch))
        SY.wait("sV", ("strow", ch))
        SY.do(lambda e: e.dma_start(out=AP(statd, 0, [[CH, 2], [1, CH]]),
                                    in_=AP(statr, 0, [[2 * CH, 1], [CH, 2], [1, CH]])), "sSY",
              key=("strst", ch), n=16)
        SY.wait("sSY", ("strst", ch))
        SY.do(lambda e: e.dma_start(
            out=RM[:, :], in_=AP(statd, 0, [[0, 128], [CH, 2], [1, CH]])),
            "sSY", key=("RMld", ch), n=16)
    def emit_rest(ch):
        par = ch % 2
        for dc in range(DC):
            sl = dc % 2
            if dc < 2:
                G.wait("sT", ("s1mm", ch, DC - 2 + dc))
                G.wait("sS", ("sq", ch, DC - 2 + dc))
                G.wait("sT", ("s2mm", ch, DC - 2 + dc))
            else:
                G.wait("sV", ("hb", ch, dc - 2))
            G.do(lambda e, ch=ch, dc=dc, sl=sl: e.dma_start(
                out=xbt[sl][:, :],
                in_=AP(xT, dc * 128 * L + ch * CH, [[L, 128], [1, CH]])),
                "sGD", key=("xbh", ch, dc), n=16)
            V.wait("sGD", ("xbh", ch, dc))
            if dc == 0:
                V.wait("sSY", ("RMld", ch))
                V.wait("sV", "mods")
                if ch >= 1:
                    V.wait("sT", ("bcmm", ch - 1, DC - 1))
                    V.wait("sT", ("dtmmL", ch - 1, DCH - 1))
                    V.wait("sG", ("w", ch - 1, DCH - 1))
            V.do(lambda e, sl=sl: e.tensor_mul(lnw0[:, :], xbt[sl][:, :], RM[:, 0:CH]))
            V.do(lambda e: e.tensor_sub(lnw0[:, :], lnw0[:, :], RM[:, CH:2 * CH]))
            V.do(lambda e, dc=dc: e.tensor_scalar(
                out=hb[:, dc * CH:(dc + 1) * CH], in0=lnw0[:, :],
                scalar1=mods[:, dc:dc + 1], scalar2=mods[:, DC + dc:DC + dc + 1],
                op0=ALU.mult, op1=ALU.add), "sV", key=("hb", ch, dc))
        for dc in range(DC):
            T.wait("sV", ("hb", ch, dc))
            if dc == 0 and ch >= 1:
                T.wait("sV", ("bcev", ch - 1))
            T.do(lambda e, dc=dc: e.matmul(
                pb[4][0:32, :], wbc_b[:, dc * 32:(dc + 1) * 32], hb[:, dc * CH:(dc + 1) * CH],
                start=(dc == 0), stop=(dc == DC - 1)), "sT", key=("bcmm", ch, dc))
        V.wait("sT", ("bcmm", ch, DC - 1))
        if ch >= 1:
            V.wait("sSY", ("BbCb", ch - 1, NST - 1))
        V.do(lambda e: e.tensor_copy(out=bcv[:, :], in_=pb[4][0:32, :]), "sV", key=("bcev", ch))
        SY.wait("sV", ("bcev", ch))
        SY.do(lambda e: e.dma_start(out=bcd.ap(), in_=bcv[:, :]), "sSY", key=("bcst", ch), n=16)
        for ec in range(DCH):
            dsl = ec % 2
            if ec == 0 and ch == 0:
                SY.wait("sSY", ("dtwst", 2 * DC - 1))
            pk = lin(ch, ec, 1, DCH)
            if pk is not None:
                SY.wait("sT", ("dtmmL", pk[0], pk[1]))
            SY.do(lambda e, ec=ec, dsl=dsl: e.dma_start(
                out=AP(dtw_b[dsl], 0, [[DC * 128, 128], [128, DC], [1, 128]]),
                in_=AP(dtw_d, ec * TILE, [[128, 128], [DCH * TILE, DC], [1, 128]])),
                "sSY", key=("dtwld", ch, ec), n=16)
            bank = 2 + (ec % 2)
            for dc in range(DC):
                if dc == 0:
                    T.wait("sSY", ("dtwld", ch, ec))
                    T.wait("sV", ("hb", ch, dc))
                    pk = lin(ch, ec, 2, DCH)
                    if pk is not None:
                        T.wait("sS", ("dtexp", pk[0], pk[1]))
                T.do(lambda e, dc=dc, dsl=dsl, bank=bank: e.matmul(
                    pb[bank][:, :], dtw_b[dsl][:, dc * 128:(dc + 1) * 128],
                    hb[:, dc * CH:(dc + 1) * CH],
                    start=(dc == 0), stop=(dc == DC - 1)),
                    "sT", key=("dtmmL", ch, ec) if dc == DC - 1 else None)
            S.wait("sT", ("dtmmL", ch, ec))
            pk = lin(ch, ec, 1, DCH)
            if pk is not None:
                S.wait("sV", ("clamp2", pk[0], pk[1]))
            S.do(lambda e, ec=ec, bank=bank: e.activation(
                out=etmp[ec % 2][:, :], in_=pb[bank][:, :], func=AF.Exp,
                bias=dtbc[:, ec:ec + 1]), "sS", key=("dtexp", ch, ec))
            V.wait("sS", ("dtexp", ch, ec))
            V.do(lambda e, ec=ec: e.tensor_scalar_min(
                etmp[ec % 2][:, :], etmp[ec % 2][:, :], 1e30), "sV", key=("clamp2", ch, ec))
            S.wait("sV", ("clamp2", ch, ec))
            S.do(lambda e, ec=ec: e.activation(
                out=delta[:, ec * CH:(ec + 1) * CH], in_=etmp[ec % 2][:, :],
                func=AF.Ln, bias=1.0), "sS", key=("sp", ch, ec))
        for ec in range(DCH):
            G.wait("sS", ("sp", ch, ec))
            G.do(lambda e, ec=ec: e.tensor_mul(
                wsc[:, ec * CH:(ec + 1) * CH], delta[:, ec * CH:(ec + 1) * CH],
                hb[:, ec * CH:(ec + 1) * CH]), "sG", key=("w", ch, ec))
        NLI = NST * DCH
        li0 = ch * NLI

        def emit_bcb(n):
            bsl = n % 3
            if n == 0:
                SY.wait("sSY", ("bcst", ch))
            pk = lin(ch, n, 3, NST)
            if pk is not None:
                SY.wait("sV", ("cm", pk[0], pk[1], DCH - 1))
                SY.wait("sG", ("tree", pk[0], pk[1], DCH - 1))
            SY.do(lambda e, n=n, bsl=bsl: e.dma_start(
                out=BbCb[bsl][:, :],
                in_=AP(bcd, n * CH, [[0, 128], [NST * CH, 2], [1, CH]])),
                "sSY", key=("BbCb", ch, n), n=16)

        def emit_u(li):
            n = (li - li0) // DCH
            dc = (li - li0) % DCH
            i = li % 3
            bsl = n % 3
            pk2 = li - 3

            G.wait("sSY", ("BbCb", ch, n)) if dc == 0 else None
            if pk2 >= 0:
                G.wait("sV", ("scanH", pk2))
            G.do(lambda e, i=i, dc=dc, bsl=bsl: e.tensor_tensor(
                out=ubuf[i][:, :], in0=wsc[:, dc * CH:(dc + 1) * CH],
                in1=BbCb[bsl][:, 0:CH], op=ALU.mult), "sG", key=("u", li))

        for n in range(NST):
            bsl = n % 3
            if n == 0:
                emit_bcb(0)
            if n + 1 < NST:
                emit_bcb(n + 1)
            for dc in range(DCH):
                li = li0 + n * DCH + dc
                i = li % 3
                ih = li % 2
                pk2 = li - 3
                S.wait("sS", ("sp", ch, dc)) if n == 0 else None
                if pk2 >= 0:
                    S.wait("sG", ("treeL", pk2))
                S.do(lambda e, n=n, dc=dc, i=i: e.activation(
                    out=abuf[i][:, :], in_=delta[:, dc * CH:(dc + 1) * CH], func=AF.Exp,
                    scale=acol[:, dc * NST + n:dc * NST + n + 1]),
                    "sS", key=("aexp", li))
                if n == 0 and dc == 0:
                    emit_u(li)
                    emit_u(li + 1)
                if li + 2 <= li0 + NLI - 1:
                    emit_u(li + 2)
                V.wait("sS", ("aexp", li))
                V.wait("sG", ("u", li))

                def scan_op(e, ch=ch, n=n, dc=dc, i=i, ih=ih):
                    init = 0.0 if ch == 0 else hst[:, dc * NST + n:dc * NST + n + 1]
                    return e.tensor_tensor_scan(
                        out=Hbuf[ih][:, :], data0=abuf[i][:, :], data1=ubuf[i][:, :],
                        initial=init, op0=ALU.mult, op1=ALU.add)
                V.do(scan_op, "sV", key=("scanH", li))
                if pk2 >= 0:
                    V.wait("sG", ("treeL", pk2))
                V.do(lambda e, i=i, ih=ih, bsl=bsl: e.tensor_mul(
                    abuf[i][:, :], Hbuf[ih][:, :], BbCb[bsl][:, CH:2 * CH]),
                    "sV", key=("cm", ch, n, dc))
                V.do(lambda e, n=n, dc=dc, ih=ih: e.tensor_copy(
                    out=hst[:, dc * NST + n:dc * NST + n + 1],
                    in_=Hbuf[ih][:, CH - 1:CH]))
                G.wait("sV", ("cm", ch, n, dc))
                if n == 0:
                    if ch >= 1:
                        G.wait("sGD", ("rsst", ch - 1, 3)) if dc == 0 else None
                    G.do(lambda e, dc=dc, i=i: e.tensor_copy(
                        out=yacc[:, dc * CH:(dc + 1) * CH], in_=abuf[i][:, :]),
                        "sG", key=("tree", ch, n, dc))
                else:
                    G.do(lambda e, dc=dc, i=i: e.tensor_tensor(
                        out=yacc[:, dc * CH:(dc + 1) * CH],
                        in0=yacc[:, dc * CH:(dc + 1) * CH], in1=abuf[i][:, :],
                        op=ALU.add), "sG", key=("tree", ch, n, dc))
                mile[("sG", ("treeL", li))] = cnt["sG"]
        if ch >= 2:
            G.wait("sCC", ("rs", ch - 2))
        for half in range(2):
            mcol = half
            S.wait("sG", ("treeL", ch * NST * DCH + NST * DCH - 1)) if half == 0 else None
            if half == 0 and ch >= 1:
                S.wait("sGD", ("rsst", ch - 1, 3))
            if half == 1:
                S.wait("sGD", ("rsst", ch, 1))
            S.do(lambda e, mcol=mcol: e.activation(
                out=hsb[:, 0:DCH * CH], in_=yacc[:, :], func=AF.Copy,
                scale=maskt[:, mcol:mcol + 1]),
                "sS", key=(("ylo", ch) if mcol == 0 else ("yhi", ch)))
            G.wait("sS", ("ylo", ch) if half == 0 else ("yhi", ch))
            for s in range(2):
                G.do(lambda e, s=s, par=par, half=half: e.dma_start(
                    out=AP(rs_in[par], s * D * HT + half * EH * HT,
                           [[HT, 128], [128 * HT, DCH], [1, HT]]),
                    in_=AP(hsb, s * HT, [[KCI * HT, 128], [CH, DCH], [1, HT]])),
                    "sGD", key=("rsst", ch, 2 * half + s), n=16)
        G.wait("sGD", ("rsst", ch, 3))
        G.do(lambda e, par=par: e.collective_compute(
            "ReduceScatter", ALU.add,
            replica_groups=[[0, 1], [2, 3], [4, 5], [6, 7]],
            ins=[rs_in[par].ap()], outs=[rs_out[ch].ap()]), "sCC", key=("rs", ch))

    def emit_unit(u):
        par = u % 2
        lastch = RNCH - 1
        SY.wait("sCC", ("rs", u))
        if u >= 1:
            SY.wait("sT", ("opmmL", u - 1))
        SY.do(lambda e, u=u: e.dma_start(
            out=yDt[:, :], in_=AP(rs_out[u], 0, [[HT, 128], [128 * HT, DC], [1, HT]])),
            "sSY", key=("yD", u), n=16)
        if u >= 1:
            G.wait("sV", ("xb2", u - 1, DC - 1))
        G.do(lambda e, u=u: e.dma_start(
            out=xbD[:, :], in_=AP(xDr, u * HT, [[NCH * HT, 128], [128 * NCH * HT, DC], [1, HT]])),
            "sGD", key=("xbD", u), n=16)
        for dc in range(DC):
            T.wait("sGD", ("xbD", u)) if dc == 0 else None
            if dc == 0:
                if u == 0:
                    T.wait("sV", ("var", lastch))
                else:
                    T.wait("sV", ("var2", u - 1))
                    T.wait("sS", ("gev", u - 1, KCI - 1))
                    T.wait("sV", ("hm", u - 1, KCI - 1))
            T.do(lambda e, dc=dc: e.matmul(
                pb[0][0:1, 0:HT], onesb[:, :], xbD[:, dc * HT:(dc + 1) * HT],
                start=(dc == 0), stop=(dc == DC - 1)), "sT", key=("s1mmu", u, dc))
            S.wait("sGD", ("xbD", u)) if dc == 0 else None
            if dc >= 2:
                S.wait("sT", ("s2mmu", u, dc - 2))
            elif u == 0:
                S.wait("sT", ("s2mm", lastch, DC - 2 + dc))
            else:
                S.wait("sT", ("s2mm2", u - 1, DC - 2 + dc))
            S.do(lambda e, dc=dc: e.activation(
                out=sqt[dc % 2][:, 0:HT], in_=xbD[:, dc * HT:(dc + 1) * HT],
                func=AF.Square), "sS", key=("squ", u, dc))
            T.wait("sS", ("squ", u, dc))
            T.do(lambda e, dc=dc: e.matmul(
                pb[1][0:1, 0:HT], onesb[:, :], sqt[dc % 2][:, 0:HT],
                start=(dc == 0), stop=(dc == DC - 1)), "sT", key=("s2mmu", u, dc))
        V.wait("sT", ("s1mmu", u, DC - 1))
        V.wait("sT", ("s2mmu", u, DC - 1))
        if u == 0:
            V.wait("sS", ("rstd", lastch))
        else:
            V.wait("sS", ("rstd2", u - 1))
        V.do(lambda e: e.tensor_scalar_mul(srow[0:1, 0:HT], pb[0][0:1, 0:HT], 1.0 / D))
        V.do(lambda e: e.tensor_scalar_mul(srow[0:1, CH:CH + HT], pb[1][0:1, 0:HT], 1.0 / D))
        V.do(lambda e: e.tensor_mul(statr2[0:1, HT:2 * HT], srow[0:1, 0:HT], srow[0:1, 0:HT]))
        V.do(lambda e: e.tensor_sub(srow[0:1, CH:CH + HT], srow[0:1, CH:CH + HT],
                                    statr2[0:1, HT:2 * HT]),
             "sV", key=("varu", u))
        S.wait("sV", ("varu", u))
        if u == 0:
            S.wait("sSY", ("strst", lastch))
        else:
            S.wait("sSY", ("st2b", u - 1))
        S.do(lambda e: e.activation(out=statr2[0:1, 0:HT], in_=srow[0:1, CH:CH + HT],
                                    func=AF.Ln, bias=1e-5))
        S.do(lambda e: e.activation(out=statr2[0:1, 0:HT], in_=statr2[0:1, 0:HT],
                                    func=AF.Exp, scale=-0.5),
             "sS", key=("rstdu", u))
        V.wait("sS", ("rstdu", u))
        V.do(lambda e: e.tensor_mul(statr2[0:1, HT:2 * HT], srow[0:1, 0:HT], statr2[0:1, 0:HT]),
             "sV", key=("strowu", u))
        SY.wait("sV", ("strowu", u))
        SY.do(lambda e: e.dma_start(out=AP(statd2, 0, [[HT, 2], [1, HT]]),
                                    in_=AP(statr2, 0, [[2 * HT, 1], [HT, 2], [1, HT]])), "sSY",
              key=("st2a", u), n=16)
        SY.wait("sSY", ("st2a", u))
        SY.do(lambda e: e.dma_start(
            out=RM2[:, :], in_=AP(statd2, 0, [[0, 128], [HT, 2], [1, HT]])),
            "sSY", key=("RM2", u), n=16)
        for mc in range(DC):
            osl = mc % 2
            if mc >= 1:
                SY.wait("sT", ("opmm", u, mc - 1))
            elif u >= 1:
                SY.wait("sT", ("opmm", u - 1, DC - 1))
            else:
                SY.wait("sSY", ("opwst", DC - 1))
            SY.do(lambda e, mc=mc, osl=osl: e.dma_start(
                out=AP(opw_b[osl], 0, [[DC * 128, 128], [1, DC * 128]]),
                in_=AP(opw_d, mc * DC * TILE, [[DC * 128, 128], [1, DC * 128]])),
                "sSY", key=("opwld2", u, mc), n=16)
            bank = 5 + (mc % 2)
            for kc in range(DC):
                if kc == 0:
                    T.wait("sSY", ("opwld2", u, mc))
                    T.wait("sSY", ("yD", u)) if mc == 0 else None
                    if mc >= 2:
                        T.wait("sS", ("relu", u, mc - 2))
                    elif u >= 1:
                        T.wait("sV", ("dnev", u - 1, DC - 2 + mc))
                    elif u == 0:
                        if mc == 0:
                            T.wait("sV", ("modev", 23))
                        else:
                            T.wait("sS", ("dtwev", 2 * DC - 1))
                T.do(lambda e, kc=kc, osl=osl, bank=bank: e.matmul(
                    pb[bank][:, 0:HT], opw_b[osl][:, kc * 128:(kc + 1) * 128],
                    yDt[:, kc * HT:(kc + 1) * HT],
                    start=(kc == 0), stop=(kc == DC - 1)),
                    "sT", key=("opmm", u, mc) if kc == DC - 1 else None)
            S.wait("sT", ("opmm", u, mc))
            if mc >= 2:
                S.wait("sV", ("xb2", u, mc - 2))
            S.do(lambda e, bank=bank, osl=osl, mc=mc: e.activation(
                out=m1t[osl][:, :], in_=pb[bank][:, 0:HT], func=AF.Relu,
                bias=opbc[:, mc:mc + 1]), "sS", key=("relu", u, mc))
            V.wait("sSY", ("RM2", u)) if mc == 0 else None
            V.wait("sS", ("relu", u, mc))
            V.do(lambda e, mc=mc: e.tensor_mul(
                lnw0[:, 0:HT], xbD[:, mc * HT:(mc + 1) * HT], RM2[:, 0:HT]))
            V.do(lambda e: e.tensor_sub(lnw0[:, 0:HT], lnw0[:, 0:HT], RM2[:, HT:2 * HT]))
            V.do(lambda e, mc=mc: e.tensor_scalar(
                out=hDt[:, :], in0=lnw0[:, 0:HT],
                scalar1=cap1g[:, mc:mc + 1], scalar2=modg[:, mc:mc + 1],
                op0=ALU.mult, op1=ALU.add))
            V.do(lambda e, mc=mc, osl=osl: e.scalar_tensor_tensor(
                out=m1t[osl][:, :], in0=hDt[:, :], scalar=dvecc[:, mc:mc + 1],
                in1=m1t[osl][:, :], op0=ALU.mult, op1=ALU.add))
            if mc >= 2:
                G.wait("sV", ("xb2", u, mc - 2))
            G.do(lambda e, u=u, mc=mc, osl=osl: e.dma_start(
                out=xdf[osl][:, :],
                in_=AP(xDr, mc * 128 * NCH * HT + u * HT, [[NCH * HT, 128], [1, HT]])),
                "sSY", key=("xdf", u, mc), n=16)
            V.wait("sSY", ("xdf", u, mc))
            if mc >= 2:
                V.wait("sSY", ("x2st", u, mc - 1))
            V.do(lambda e, mc=mc, osl=osl: e.scalar_tensor_tensor(
                out=x2t[osl][:, :], in0=m1t[osl][:, :], scalar=gacol[:, mc:mc + 1],
                in1=xdf[osl][:, :], op0=ALU.mult, op1=ALU.add))
            if mc >= 2:
                V.wait("sT", ("s1mm2", u, mc - 2))
                V.wait("sS", ("sq2", u, mc - 2))
            V.do(lambda e, osl=osl: e.tensor_copy(
                out=xb2[osl][:, :], in_=x2t[osl][:, :]),
                "sV", key=("xb2", u, mc))
            G.wait("sV", ("xb2", u, mc))
            G.do(lambda e, u=u, mc=mc, osl=osl: e.dma_start(
                out=AP(x2d, mc * 128 * NCH * HT + u * HT, [[NCH * HT, 128], [1, HT]]),
                in_=x2t[osl][:, :]), "sSY", key=("x2st", u, mc), n=16)
            T.wait("sV", ("xb2", u, mc))
            if mc == 0:
                T.wait("sV", ("varu", u))
            T.do(lambda e, mc=mc, osl=osl: e.matmul(
                pb[0][0:1, 0:HT], onesb[:, :], xb2[osl][:, :],
                start=(mc == 0), stop=(mc == DC - 1)), "sT", key=("s1mm2", u, mc))
            if mc >= 2:
                S.wait("sT", ("s2mm2", u, mc - 2))
            S.wait("sV", ("xb2", u, mc))
            S.do(lambda e, mc=mc, osl=osl: e.activation(
                out=sqt[osl][:, 0:HT], in_=xb2[osl][:, :],
                func=AF.Square), "sS", key=("sq2", u, mc))
            T.wait("sS", ("sq2", u, mc))
            T.do(lambda e, mc=mc, osl=osl: e.matmul(
                pb[1][0:1, 0:HT], onesb[:, :], sqt[osl][:, 0:HT],
                start=(mc == 0), stop=(mc == DC - 1)), "sT", key=("s2mm2", u, mc))
        mile[("sT", ("opmmL", u))] = cnt["sT"]
        V.wait("sT", ("s1mm2", u, DC - 1))
        V.wait("sT", ("s2mm2", u, DC - 1))
        V.wait("sSY", ("RM2", u))
        V.do(lambda e: e.tensor_scalar_mul(srow[0:1, 0:HT], pb[0][0:1, 0:HT], 1.0 / D))
        V.do(lambda e: e.tensor_scalar_mul(srow[0:1, CH:CH + HT], pb[1][0:1, 0:HT], 1.0 / D))
        V.do(lambda e: e.tensor_mul(statr2[0:1, HT:2 * HT], srow[0:1, 0:HT], srow[0:1, 0:HT]))
        V.do(lambda e: e.tensor_sub(srow[0:1, CH:CH + HT], srow[0:1, CH:CH + HT],
                                    statr2[0:1, HT:2 * HT]),
             "sV", key=("var2", u))
        S.wait("sV", ("var2", u))
        S.wait("sSY", ("st2a", u))
        S.do(lambda e: e.activation(out=statr2[0:1, 0:HT], in_=srow[0:1, CH:CH + HT],
                                    func=AF.Ln, bias=1e-5))
        S.do(lambda e: e.activation(out=statr2[0:1, 0:HT], in_=statr2[0:1, 0:HT],
                                    func=AF.Exp, scale=-0.5),
             "sS", key=("rstd2", u))
        V.wait("sS", ("rstd2", u))
        V.do(lambda e: e.tensor_mul(statr2[0:1, HT:2 * HT], srow[0:1, 0:HT], statr2[0:1, 0:HT]),
             "sV", key=("strow2", u))
        SY.wait("sV", ("strow2", u))
        SY.do(lambda e: e.dma_start(out=AP(statd2, 0, [[HT, 2], [1, HT]]),
                                    in_=AP(statr2, 0, [[2 * HT, 1], [HT, 2], [1, HT]])), "sSY",
              key=("st2b", u), n=16)
        SY.wait("sSY", ("st2b", u))
        SY.do(lambda e: e.dma_start(
            out=RM2[:, :], in_=AP(statd2, 0, [[0, 128], [HT, 2], [1, HT]])),
            "sSY", key=("RM22", u), n=16)
        for dc in range(DC):
            if dc >= 2:
                G.wait("sV", ("h2b", u, dc - 2))
            G.wait("sSY", ("x2st", u, DC - 1))
            G.do(lambda e, u=u, dc=dc: e.dma_start(
                out=xh2[dc % 2][:, :],
                in_=AP(x2d, dc * 128 * NCH * HT + u * HT, [[NCH * HT, 128], [1, HT]])),
                "sGD", key=("xh2", u, dc), n=16)
            V.wait("sGD", ("xh2", u, dc))
            if dc == 0:
                V.wait("sSY", ("RM22", u))
                if u >= 1:
                    V.wait("sT", ("gumm", u - 1, 2 * KCI - 1))
            V.do(lambda e, dc=dc: e.tensor_mul(
                lnw0[:, 0:HT], xh2[dc % 2][:, :], RM2[:, 0:HT]))
            V.do(lambda e: e.tensor_sub(lnw0[:, 0:HT], lnw0[:, 0:HT], RM2[:, HT:2 * HT]))
            V.do(lambda e, dc=dc: e.tensor_scalar(
                out=h2b[:, dc * HT:(dc + 1) * HT], in0=lnw0[:, 0:HT],
                scalar1=cbp1[:, dc:dc + 1], scalar2=modg[:, 3 * DC + dc:3 * DC + dc + 1],
                op0=ALU.mult, op1=ALU.add), "sV", key=("h2b", u, dc))
        for ic in range(KCI):
            gbank = 2 if ic % 2 == 0 else 0
            ubank = 3 if ic % 2 == 0 else 1
            for gu in range(2):
                wsl = (2 * ic + gu) % 3
                wdd = wg_d if gu == 0 else wu_d
                j = 2 * ic + gu
                if u == 0 and j == 0:
                    S.wait("sSY", ("shuf", 2, KCI - 1))
                if j >= 3:
                    S.wait("sT", ("gumm", u, j - 3))
                elif u >= 1:
                    S.wait("sT", ("dnmm", u - 1, DC - 1, 2))
                S.do(lambda e, ic=ic, wsl=wsl, wdd=wdd: e.dma_start(
                    out=AP(wf_b[wsl], 0, [[DC * 128, 128], [1, DC * 128]]),
                    in_=AP(wdd, ic * DC * TILE, [[DC * 128, 128], [1, DC * 128]])),
                    "sSY", key=("wfld", u, j), n=16)
                bank = gbank if gu == 0 else ubank
                for kc in range(DC):
                    if kc == 0:
                        T.wait("sSY", ("wfld", u, j))
                        T.wait("sV", ("h2b", u, DC - 1)) if j == 0 else None
                        if gu == 0:
                            if ic >= 2:
                                T.wait("sS", ("gev", u, ic - 2))
                            elif ic == 1:
                                T.wait("sV", ("var2", u))
                            elif u >= 1:
                                T.wait("sS", ("gev", u - 1, KCI - 1))
                            else:
                                T.wait("sS", ("dtexp", RNCH - 1, DCH - 2))
                        else:
                            if ic >= 2:
                                T.wait("sV", ("hm", u, ic - 2))
                            elif ic == 1:
                                T.wait("sV", ("var2", u))
                            elif u >= 1:
                                T.wait("sV", ("hm", u - 1, KCI - 1))
                            else:
                                T.wait("sS", ("dtexp", RNCH - 1, DCH - 1))
                    T.do(lambda e, kc=kc, wsl=wsl, bank=bank: e.matmul(
                        pb[bank][:, 0:HT], wf_b[wsl][:, kc * 128:(kc + 1) * 128],
                        h2b[:, kc * HT:(kc + 1) * HT],
                        start=(kc == 0), stop=(kc == DC - 1)),
                        "sT", key=("gumm", u, j) if kc == DC - 1 else None)
            S.wait("sT", ("gumm", u, 2 * ic))
            if ic >= 2:
                S.wait("sV", ("hm", u, ic - 2))
            elif ic == 0 and u >= 1:
                S.wait("sV", ("hm", u - 1, KCI - 1))
            S.do(lambda e, ic=ic, gbank=gbank: e.activation(
                out=gtt[ic % 2][:, :], in_=pb[gbank][:, 0:HT], func=AF.Silu),
                "sS", key=("gev", u, ic))
            V.wait("sS", ("gev", u, ic))
            V.wait("sT", ("gumm", u, 2 * ic + 1))
            if ic == 0 and u >= 1:
                V.wait("sT", ("dnmm", u - 1, DC - 1, 2))
            if ic == 0 and u == 0:
                V.wait("sGD", ("rsst", RNCH - 1, 3))
            V.do(lambda e, ic=ic, ubank=ubank: e.tensor_mul(
                hsb[:, ic * HT:(ic + 1) * HT], gtt[ic % 2][:, :], pb[ubank][:, 0:HT]),
                "sV", key=("hm", u, ic))
        for dc in range(DC):
            bank = 5 + (dc % 2)
            for kg in range(3):
                kn = 16 if kg < 2 else KCI - 32
                jd = 2 * KCI + dc * 3 + kg
                wsl = jd % 3
                jp = jd - 3
                if jp >= 2 * KCI:
                    q = jp - 2 * KCI
                    SY.wait("sT", ("dnmm", u, q // 3, q % 3))
                else:
                    SY.wait("sT", ("gumm", u, 2 * KCI - 1))
                SY.do(lambda e, dc=dc, kg=kg, kn=kn, wsl=wsl: e.dma_start(
                    out=AP(wf_b[wsl], 0, [[DC * 128, 128], [1, kn * 128]]),
                    in_=AP(wd_d, dc * KCI * TILE + kg * 16 * 128,
                           [[KCI * 128, 128], [1, kn * 128]])),
                    "sSY", key=("wdld", u, dc, kg), n=16)
                for kk in range(kn):
                    kc = kg * 16 + kk
                    if kk == 0:
                        T.wait("sSY", ("wdld", u, dc, kg))
                        if kg == 0:
                            T.wait("sV", ("hm", u, KCI - 1)) if dc == 0 else None
                            if dc >= 2:
                                T.wait("sV", ("dnev", u, dc - 2))
                            elif dc == 0:
                                T.wait("sS", ("relu", u, DC - 2))
                            else:
                                T.wait("sS", ("relu", u, DC - 1))
                    T.do(lambda e, kc=kc, kk=kk, wsl=wsl, bank=bank: e.matmul(
                        pb[bank][:, 0:HT], wf_b[wsl][:, kk * 128:(kk + 1) * 128],
                        hsb[:, kc * HT:(kc + 1) * HT],
                        start=(kc == 0), stop=(kc == KCI - 1)),
                        "sT", key=("dnmm", u, dc, kg) if kk == kn - 1 else None)
            if dc >= 4:
                G.wait("sV", ("dnev", u, dc - 4))
            if dc == 0:
                G.wait("sSY", ("x2st", u, DC - 1))
            G.do(lambda e, u=u, dc=dc: e.dma_start(
                out=x2sl[dc % 4][:, 0:HT],
                in_=AP(x2d, dc * 128 * NCH * HT + u * HT, [[NCH * HT, 128], [1, HT]])),
                "sSY", key=("x2ld", u, dc), n=16)
            V.wait("sT", ("dnmm", u, dc, 2))
            V.wait("sSY", ("x2ld", u, dc))
            if dc >= 2 and dc % 2 == 0:
                V.wait("sST", ("x3st", u, dc // 2 - 1))
            if dc == 0 and u >= 1:
                V.wait("sST", ("x3st", u - 1, 7))
            V.do(lambda e, dc=dc, bank=bank: e.scalar_tensor_tensor(
                out=x3s[:, (dc % 2) * HT:(dc % 2 + 1) * HT], in0=pb[bank][:, 0:HT],
                scalar=gbcol[:, dc:dc + 1], in1=x2sl[dc % 4][:, 0:HT],
                op0=ALU.mult, op1=ALU.add), "sV", key=("dnev", u, dc))
            if dc % 2 == 1:
                q = dc // 2
                G.wait("sV", ("dnev", u, dc))
                G.do(lambda e, u=u, q=q: e.dma_start(
                    out=AP(x3d, q * 2 * 128 * NCH * HT + u * HT,
                           [[NCH * HT, 128], [128 * NCH * HT, 2], [1, HT]]),
                    in_=AP(x3s, 0, [[2 * HT, 128], [HT, 2], [1, HT]])),
                    "sST", key=("x3st", u, q), n=16)

    for ch in range(RNCH):
        if SAFE:
            barrier()
        emit_stats(ch)
        emit_rest(ch)
        if 3 <= ch <= 5:
            emit_shuf_group(ch - 3)
    for g in range(_shuf_state[0], 3):
        emit_shuf_group(g)
    for u in range(RNCH):
        if SAFE:
            barrier()
        emit_unit(u)
    SY.wait_val("sST", cnt["sST"])
    for _dn in sorted(cnt):
        if _dn.startswith("d_"):
            SY.wait_val(_dn, cnt[_dn])

    with es:
        with nc.Block() as block:
            @block.tensor
            def _(e):
                for f in ops["tensor"]:
                    f(e)

            @block.vector
            def _(e):
                for f in ops["vector"]:
                    f(e)

            @block.scalar
            def _(e):
                for f in ops["scalar"]:
                    f(e)

            @block.gpsimd
            def _(e):
                for f in ops["gpsimd"]:
                    f(e)

            @block.sync
            def _(e):
                for f in ops["sync"]:
                    f(e)

    return nc


_NC_CACHE = [None]
_LAST_RES = [None]


def kernel(L0_attn_hidden_states, L0_attn_x_proj_weight, L0_attn_dt_proj_weight,
           L0_attn_dt_proj_bias, L0_attn_A_log, L0_attn_D, L0_attn_out_proj_weight,
           L0_attn_out_proj_bias, L0_ffn_lut, L0_ffn_gate_idx, L0_ffn_up_idx,
           L0_ffn_down_idx, L0_c, L0_wmod):
    x = np.asarray(L0_attn_hidden_states, dtype=np.float32)
    xpw_ = np.asarray(L0_attn_x_proj_weight, np.float32)
    dtw_ = np.asarray(L0_attn_dt_proj_weight, np.float32)
    dtb_ = np.asarray(L0_attn_dt_proj_bias, np.float32)
    alog_ = np.asarray(L0_attn_A_log, np.float32)
    dv_ = np.asarray(L0_attn_D, np.float32)
    opw_ = np.asarray(L0_attn_out_proj_weight, np.float32)
    opb_ = np.asarray(L0_attn_out_proj_bias, np.float32)
    lut_ = np.asarray(L0_ffn_lut, np.float32)
    gi_ = np.asarray(L0_ffn_gate_idx, np.int32)
    ui_ = np.asarray(L0_ffn_up_idx, np.int32)
    di_ = np.asarray(L0_ffn_down_idx, np.int32)
    c_ = np.asarray(L0_c, np.float32)
    wm_ = np.asarray(L0_wmod, np.float32)

    if _NC_CACHE[0] is None:
        _NC_CACHE[0] = build_program()
    nc = _NC_CACHE[0]

    ident = np.eye(128, dtype=ml_dtypes.bfloat16)
    in_maps = []
    for core in range(NCORE):
        b = core // 2
        h = core % 2
        sig = np.arange(D) if h == 0 else np.concatenate(
            [np.arange(EH, D), np.arange(0, EH)])
        halfsl = slice(0, EH) if h == 0 else slice(EH, D)
        masks = np.zeros((128, 8), np.float32)
        masks[:, 0] = 1.0 if h == 0 else 0.0
        masks[:, 1] = 0.0 if h == 0 else 1.0
        masks[:, 2 + b] = 1.0
        masks[:, 6] = 1.0 if h == 0 else 0.0
        masks[:, 7] = 0.0 if h == 0 else 1.0
        didx_p = np.zeros((D, 768), np.int32)
        c0 = 768 * core
        c1 = min(INTER, 768 * (core + 1))
        if c1 > c0:
            didx_p[:, :c1 - c0] = di_[:, c0:c1]
        in_maps.append({
            "xT": np.ascontiguousarray(x[b].T[sig]),
            "xD": np.ascontiguousarray(
                x[b].T.reshape(D, NCH, 2, HT)[:, :, h, :].reshape(D, NCH * HT)),
            "dtpwT": np.ascontiguousarray(dtw_.T[:, halfsl]),
            "xpw": np.ascontiguousarray(xpw_[0:128][:, sig]),
            "wbcT": np.ascontiguousarray(xpw_[128:160][:, sig].T),
            "dtbias": np.ascontiguousarray(dtb_[halfsl]),
            "alog": np.ascontiguousarray(alog_[halfsl]),
            "dvec": dv_,
            "opwT": np.ascontiguousarray(opw_.T),
            "opbias": opb_,
            "gidx": np.ascontiguousarray(gi_[:, 256 * core:256 * (core + 1)]),
            "uidx": np.ascontiguousarray(ui_[:, 256 * core:256 * (core + 1)]),
            "didx": didx_p,
            "lut": lut_,
            "wmod_s": np.ascontiguousarray(wm_[256 * core:256 * (core + 1), :]),
            "cT": np.ascontiguousarray(c_.T[256 * core:256 * (core + 1), :]),
            "ident": ident,
            "masks": masks,
        })

    res = run_bass_kernel_spmd(nc, in_maps, core_ids=list(range(NCORE)))
    _LAST_RES[0] = res

    out = np.empty((B, L, D), np.float32)
    for core in range(NCORE):
        h = core % 2
        b = core // 2
        x3 = res.results[core]["x3"].reshape(D, NCH, HT)
        for ch in range(NCH):
            out[b, ch * CH + h * HT:ch * CH + h * HT + HT, :] = x3[:, ch, :].T
    return out
